# version 14
# baseline (speedup 1.0000x reference)
import numpy as np
import ml_dtypes
from contextlib import ExitStack
import concourse.bass as bass
import concourse.mybir as mybir
from concourse.bass_utils import run_bass_kernel_spmd

F32 = mybir.dt.float32
BF16 = mybir.dt.bfloat16
AF = mybir.ActivationFunctionType
ALU = mybir.AluOpType

D = 2048
S = 4096
ST = 512
NT = S // ST
NCH = D // 128
NJT = 12288 // ST
DENOM = float(np.sqrt(2048.0) / 4096.0)
EPS = 1e-5
NSLOT = 8

SEM_ENG = {"ld": "sync", "gd": "gpsimd", "gc": "gpsimd", "sd": "scalar",
           "pe": "tensor", "v": "vector", "s": "scalar",
           "wh0": "gpsimd", "wh1": "gpsimd", "wl0": "gpsimd", "wl1": "gpsimd",
           "sm0": "sync", "sm1": "sync", "sm2": "sync", "sm3": "sync",
           "ms": "scalar", "mv": "sync",
           "wa0": "sync", "wa1": "sync", "wa2": "sync", "wa3": "sync",
           "wb0": "scalar", "wb1": "scalar", "wb2": "scalar", "wb3": "scalar",
           "wr0": "sync", "wr1": "scalar", "wr2": "sync", "wr3": "scalar",
           "os0": "scalar", "os1": "scalar"}
for _i in range(8):
    SEM_ENG[f"rs{_i}"] = "sync"
for _i in range(16):
    SEM_ENG[f"xc{_i}"] = "scalar"


def build_nc(nt=NT, debug=False):
    nc = bass.Bass("TRN2", target_bir_lowering=False)

    xT = nc.dram_tensor("xT", [D, S], F32, kind="ExternalInput").ap()
    wq = nc.dram_tensor("wq", [D, D], F32, kind="ExternalInput").ap()
    wk = nc.dram_tensor("wk", [D, D], F32, kind="ExternalInput").ap()
    wv = nc.dram_tensor("wv", [D, D], F32, kind="ExternalInput").ap()
    wo = nc.dram_tensor("wo", [D, D], F32, kind="ExternalInput").ap()
    wb = nc.dram_tensor("wb", [D, D], F32, kind="ExternalInput").ap()
    wmod = nc.dram_tensor("wmod", [D, 12288], F32, kind="ExternalInput").ap()
    rts = nc.dram_tensor("rts", [128, NCH * 128], F32, kind="ExternalInput").ap()
    ats = nc.dram_tensor("ats", [128, NCH * 128], F32, kind="ExternalInput").ap()
    bwts = nc.dram_tensor("bwts", [128, NCH * 128], F32, kind="ExternalInput").ap()
    c_in = nc.dram_tensor("c_in", [128, NCH], F32, kind="ExternalInput").ap()
    id_in = nc.dram_tensor("id_in", [128, 128], F32, kind="ExternalInput").ap()
    em_in = nc.dram_tensor("em_in", [8, 128], BF16, kind="ExternalInput").ap()
    bb_in = nc.dram_tensor("bb_in", [128, NCH], F32, kind="ExternalInput").ap()
    outT = nc.dram_tensor("outT", [D, S], F32, kind="ExternalOutput").ap()
    modscr = nc.dram_tensor("modscr", [12288], F32).ap()
    dbg = {}
    if debug:
        for nm, shape, dty in (("d_mod", [128, 96], F32), ("d_r", [128, ST], F32),
                               ("d_b", [128, ST], F32), ("d_hhi", [128, NCH * ST], BF16),
                               ("d_hlo", [128, NCH * ST], BF16),
                               ("d_vhhi", [128, NCH * ST], BF16),
                               ("d_vhlo", [128, NCH * ST], BF16),
                               ("d_x2", [128, NCH * ST], F32),
                               ("d_lg", [8, ST], F32), ("d_mf", [128, ST], BF16),
                               ("d_h2hi", [128, NCH * ST], BF16),
                               ("d_h2lo", [128, NCH * ST], BF16),
                               ("d_r2", [128, ST], F32),
                               ("d_rthi", [128, NCH * 8], BF16),
                               ("d_rtlo", [128, NCH * 8], BF16),
                               ("d_amf", [128, ST], F32)):
            dbg[nm] = nc.dram_tensor(nm, shape, dty, kind="ExternalOutput").ap()
    wnames = ["q", "k", "v", "o", "b"]
    wsrc = {"q": wq, "k": wk, "v": wv, "o": wo, "b": wb}
    wscr = {}
    for wn in wnames:
        wscr[wn] = {
            "hi": nc.dram_tensor(f"w{wn}_hi", [128, NCH, NCH, 128], BF16).ap(),
            "lo": nc.dram_tensor(f"w{wn}_lo", [128, NCH, NCH, 128], BF16).ap(),
        }

    with ExitStack() as ctx:
        def sbt(name, shape, dtype):
            return ctx.enter_context(nc.sbuf_tensor(name, shape, dtype))

        x_t = sbt("x_t", [128, NCH * ST], F32)
        h_hi = sbt("h_hi", [128, NCH * ST], BF16)
        h_lo = sbt("h_lo", [128, NCH * ST], BF16)
        vh_hi = sbt("vh_hi", [128, NCH * ST], BF16)
        vh_lo = sbt("vh_lo", [128, NCH * ST], BF16)
        ring = [sbt(f"ring{i}", [128, NCH * 128], BF16) for i in range(NSLOT)]

        a_hi = sbt("a_hi", [128, NCH * 128], BF16)
        a_lo = sbt("a_lo", [128, NCH * 128], BF16)
        bw_hi = sbt("bw_hi", [128, NCH * 128], BF16)
        bw_lo = sbt("bw_lo", [128, NCH * 128], BF16)
        rt_hi = sbt("rt_hi", [128, NCH * 128], BF16)
        rt_lo = sbt("rt_lo", [128, NCH * 128], BF16)
        c_t = sbt("c_t", [128, NCH], F32)
        bb_t = sbt("bb_t", [128, NCH], F32)
        s_t = sbt("s_t", [128, NCH], F32)
        sa_t = sbt("sa_t", [128, NCH], F32)
        c1a_t = sbt("c1a_t", [128, NCH], F32)
        ga_t = sbt("ga_t", [128, NCH], F32)
        sb_t = sbt("sb_t", [128, NCH], F32)
        c1b_t = sbt("c1b_t", [128, NCH], F32)
        gb_t = sbt("gb_t", [128, NCH], F32)
        g1_t = sbt("g1_t", [128, NCH], F32)
        g2_t = sbt("g2_t", [128, NCH], F32)
        lg_t = sbt("lg_t", [8, ST], F32)
        modv1 = sbt("modv1", [16, 128], F32)
        modsb = lg_t
        lt_t = sbt("lt_t", [128, 4 * 8], F32)
        lm_t = sbt("lm_t", [128, 4], F32)
        mt_t = sbt("mt_t", [128, 4 * 8], F32)
        m8_t = sbt("m8_t", [8, ST], BF16)
        mf_t = sbt("mf_t", [128, ST], BF16)
        emat = sbt("emat", [8, 128], BF16)
        ident = sbt("ident", [128, 128], F32)
        ones_t = sbt("ones_t", [128, 128], F32)
        zeros_t = sbt("zeros_t", [128, ST], F32)
        eps_t = sbt("eps_t", [128, 1], F32)
        q25_t = sbt("q25_t", [128, NCH], F32)
        n1_t = sbt("n1_t", [128, 1], F32)
        sneg_t = sbt("sneg_t", [128, 1], F32)
        g3_t = sbt("g3_t", [128, NCH], F32)
        g4_t = sbt("g4_t", [128, NCH], F32)
        dt_t = sbt("dt_t", [128, 4 * 8], F32)
        m2_t = sbt("m2_t", [128, ST], F32)
        ve_t = sbt("ve_t", [128, ST], F32)
        r_t = sbt("r_t", [128, ST], F32)
        mu_t = sbt("mu_t", [128, ST], F32)
        t1_t = sbt("t1_t", [128, 2 * ST], F32)
        h1f = sbt("h1f", [128, 2 * ST], F32)
        sq_t = h1f
        ktmp = sbt("ktmp", [128, 2 * ST], F32)
        kvf = sbt("kvf", [128, ST], F32)
        kvt = sbt("kvt", [128, ST], F32)
        vhf = sbt("vhf", [128, 2 * ST], F32)
        carry = sbt("carry", [128, NCH], F32)
        amf = sbt("amf", [128, ST], F32)
        am_hi = sbt("am_hi", [128, ST], BF16)
        am_lo = sbt("am_lo", [128, ST], BF16)
        tb_t = sbt("tb_t", [128, 2 * ST], F32)
        ot_t = sbt("ot_t", [128, 2 * ST], F32)
        _otb = ot_t[:, :].bitcast(BF16)
        whit = _otb[:, 0:1024]
        wlot = _otb[:, 1024:2048]
        _r5f = ring[5][:, :].bitcast(F32)
        _r6f = ring[6][:, :].bitcast(F32)
        _r7f = ring[7][:, :].bitcast(F32)
        wm_slots = {
            0: [ktmp[:, 0:512], ktmp[:, 512:1024],
                _r6f[:, 0:512], _r6f[:, 512:1024]],
            1: [vhf[:, 0:512], vhf[:, 512:1024],
                _r7f[:, 0:512], _r7f[:, 512:1024]],
        }
        wtmp_h = [tb_t[:, 0:512], tb_t[:, 512:1024],
                  _r5f[:, 0:512], _r5f[:, 512:1024]]

        pb = [ctx.enter_context(nc.psum_tensor(f"pb{i}", [128, ST], F32))
              for i in range(8)]

        import os as _os
        if _os.environ.get("SBUF_REPORT"):
            print("sbuf remaining:", nc.sbuf_bytes_remaining)
        sems = {k: ctx.enter_context(nc.semaphore(f"sem_{k}")) for k in SEM_ENG}
        block = ctx.enter_context(nc.Block())

        OPS = []
        CNT = {k: 0 for k in SEM_ENG}

        def op(eng, fn, waits=(), inc=None, amt=None):
            ws = []
            for w in waits:
                if w is None:
                    continue
                sname, val = w
                if val is None or val <= 0:
                    continue
                if SEM_ENG[sname] == eng and sname in ("pe", "v", "s", "gc"):
                    continue
                ws.append((sname, val))
            mark = None
            if inc is not None:
                if amt is None:
                    amt = 1 if inc in ("gc", "pe", "v", "s") else 16
                CNT[inc] += amt
                mark = CNT[inc]
            OPS.append((eng, fn, ws, inc, amt))
            return mark

        def chs(buf, c, width=ST):
            return buf[:, c * width:(c + 1) * width]

        m_c = op("sync", lambda e: e.dma_start(out=c_t[:, :], in_=c_in), inc="sm0")
        m_bb = op("sync", lambda e: e.dma_start(out=bb_t[:, :], in_=bb_in), inc="sm1")
        m_id = op("sync", lambda e: e.dma_start(out=ident[:, :], in_=id_in), inc="sm2")
        m_e7 = op("sync", lambda e: e.dma_start(out=emat[:, :], in_=em_in), inc="sm3")
        op("vector", lambda e: e.memset(ones_t[:, :], 1.0), inc="v")
        op("vector", lambda e: e.memset(zeros_t[:, :], 0.0), inc="v")
        m_eps = op("vector", lambda e: e.memset(eps_t[:, :], EPS), inc="v")
        op("vector", lambda e: e.memset(q25_t[:, :], 0.25), inc="v")
        m_n1 = op("vector", lambda e: e.memset(n1_t[:, :], -1.0), inc="v")
        m_sneg = op("vector", lambda e: e.memset(sneg_t[:, :], -1.0), inc="v")
        m_silu = op("scalar", lambda e: e.activation(out=s_t[:, :], in_=c_t[:, :],
                    func=AF.Silu), waits=[("sm0", m_c)], inc="s")
        op("scalar", lambda e: e.copy(bb_t[:, 0:1], bb_t[:, 0:1]),
           waits=[("sm1", m_bb)], inc="s")

        xTr = xT.rearrange("(c p) s -> p c s", p=128)
        outTr = outT.rearrange("(c p) s -> p c s", p=128)
        x_dc = [[None] * NCH for _ in range(nt)]
        for dc in range(NCH):
            m = op("scalar", lambda e, dc=dc: e.dma_start(
                out=chs(x_t, dc), in_=xTr[:, dc, 0:ST]), inc=f"xc{dc}")
            x_dc[0][dc] = (f"xc{dc}", m)

        ab_hist = []
        for nm, src, hib, lob in (("a", ats, a_hi, a_lo), ("bw", bwts, bw_hi, bw_lo),
                                  ("rt", rts, rt_hi, rt_lo)):
            for hr in range(4):
                g = len(ab_hist)
                sl = slice(hr * 512, (hr + 1) * 512)
                bi = g % 4
                wtv = wtmp_h[bi]
                qeng = "sync" if bi % 2 == 0 else "scalar"
                lw = []
                if g >= 4:
                    lw = [("s", ab_hist[g - 4][0]), ("v", ab_hist[g - 4][1])]
                mload = op(qeng, lambda e, sl=sl, wtv=wtv, src=src: e.dma_start(
                    out=wtv, in_=src[:, sl]), waits=lw, inc=f"wr{bi}")
                mhi = op("scalar", lambda e, sl=sl, wtv=wtv, hib=hib: e.copy(
                    hib[:, sl], wtv), waits=[(f"wr{bi}", mload)], inc="s")
                mlo = op("vector", lambda e, sl=sl, wtv=wtv, hib=hib, lob=lob:
                         e.tensor_tensor(lob[:, sl], wtv, hib[:, sl],
                                         ALU.subtract),
                    waits=[("s", mhi)], inc="v")
                ab_hist.append((mhi, mlo))
        m_ab_s, m_ab_v = ab_hist[-1]
        m_rtlo = m_ab_v

        NJT2 = 24
        mod_tiles = [sa_t, c1a_t, ga_t, sb_t, c1b_t, gb_t]
        jt_ev = {}
        jt_dma = {}
        chain_cnt = [0, 0]
        chain_last_mm = {0: {}, 1: {}}
        prev_mod_dma = [None]

        def emit_jt(jt):
            chain = jt % 2
            qeng = "sync" if chain == 0 else "scalar"
            mm = None
            for dc in range(NCH):
                cj = chain_cnt[chain]
                chain_cnt[chain] += 1
                cslot = cj % 4
                buf = wm_slots[chain][cslot]
                lw = []
                if cj >= 4:
                    lw.append(("pe", chain_last_mm[chain][cj - 4]))
                wmsem = f"w{'a' if chain == 0 else 'b'}{cslot}"
                mld = op(qeng, lambda e, jt=jt, dc=dc, buf=buf: e.dma_start(
                    out=buf,
                    in_=wmod[dc * 128:(dc + 1) * 128, jt * 512:(jt + 1) * 512]),
                    waits=lw, inc=wmsem)
                wts = [(wmsem, mld), ("s", m_silu)]
                if dc == 0 and jt >= 2:
                    wts.append(jt_ev[jt - 2])
                mm = op("tensor", lambda e, jt=jt, dc=dc, buf=buf: e.matmul(
                    pb[6 + jt % 2][0:1, :], s_t[:, dc:dc + 1], buf,
                    start=(dc == 0), stop=(dc == NCH - 1)),
                    waits=wts, inc="pe")
                chain_last_mm[chain][cj] = mm
            wts = [("pe", mm)]
            if prev_mod_dma[0] is not None:
                wts.append(("ms", prev_mod_dma[0]))
            mev = op("scalar", lambda e, jt=jt: e.copy(
                modsb[0:1, :], pb[6 + jt % 2][0:1, :]), waits=wts, inc="s")
            jt_ev[jt] = ("s", mev)
            mdma = op("scalar", lambda e, jt=jt: e.dma_start(
                out=modscr[jt * ST:(jt + 1) * ST], in_=modsb[0:1, :]),
                waits=[], inc="ms")
            prev_mod_dma[0] = mdma
            jt_dma[jt] = mdma

        hr_marks = []
        wsplit_done = {}

        def emit_unit(u):
            wn, kc, hr = u
            n_hr = len(hr_marks)
            bi = n_hr % 4
            wtv = wtmp_h[bi]
            sl = slice((n_hr % 2) * 512, (n_hr % 2 + 1) * 512)
            qeng = "sync" if bi % 2 == 0 else "scalar"
            wts = []
            if n_hr >= 4:
                wts += [("s", hr_marks[n_hr - 4][0]),
                        ("v", hr_marks[n_hr - 4][1])]
            else:
                wts += [("s", m_ab_s), ("v", m_ab_v)]
            mload = op(qeng, lambda e, wn=wn, kc=kc, hr=hr, wtv=wtv: e.dma_start(
                out=wtv,
                in_=wsrc[wn][kc * 128:(kc + 1) * 128, hr * 512:(hr + 1) * 512]),
                waits=wts, inc=f"wr{bi}")
            wts_h = [(f"wr{bi}", mload)]
            if n_hr >= 2:
                wts_h.append(hr_marks[n_hr - 2][2])
            mhi = op("scalar", lambda e, sl=sl, wtv=wtv: e.copy(
                whit[:, sl], wtv), waits=wts_h, inc="s")
            wts_l = [("s", mhi)]
            if n_hr >= 2:
                wts_l.append(hr_marks[n_hr - 2][3])
            mlo = op("vector", lambda e, sl=sl, wtv=wtv: e.tensor_tensor(
                wlot[:, sl], wtv, whit[:, sl], ALU.subtract),
                waits=wts_l, inc="v")
            mcs = slice(hr * 4, (hr + 1) * 4)
            hb = n_hr % 2
            mwr_hi = op("gpsimd", lambda e, wn=wn, kc=kc, mcs=mcs, sl=sl:
                        e.dma_start(out=wscr[wn]["hi"][:, mcs, kc, :],
                        in_=whit[:, sl].rearrange("p (m j) -> p m j", m=4)),
                        waits=[("s", mhi)], inc=f"wh{hb}")
            mwr_lo = op("gpsimd", lambda e, wn=wn, kc=kc, mcs=mcs, sl=sl:
                        e.dma_start(out=wscr[wn]["lo"][:, mcs, kc, :],
                        in_=wlot[:, sl].rearrange("p (m j) -> p m j", m=4)),
                        waits=[("v", mlo)], inc=f"wl{hb}")
            hr_marks.append((mhi, mlo, (f"wh{hb}", mwr_hi), (f"wl{hb}", mwr_lo)))
            if (kc, hr) == (NCH - 1, 3):
                wsplit_done[wn] = [("wh0", CNT["wh0"]), ("wh1", CNT["wh1"]),
                                   ("wl0", CNT["wl0"]), ("wl1", CNT["wl1"])]

        units = [(wn, kc, hr) for wn in wnames for kc in range(NCH)
                 for hr in range(4)]
        INTERLEAVE = True
        u_i = 0
        for jt in range(NJT2):
            emit_jt(jt)
            tgt = (jt + 1) * len(units) // NJT2
            while u_i < tgt:
                emit_unit(units[u_i])
                u_i += 1

        prep_clear = [("s", hr_marks[-1][0]), ("v", hr_marks[-1][1]),
                      ("s", hr_marks[-2][0]), ("v", hr_marks[-2][1]),
                      ("pe", chain_last_mm[0][chain_cnt[0] - 1]),
                      ("pe", chain_last_mm[1][chain_cnt[1] - 1])]

        modscr_v = modscr.rearrange("(v c p) -> v c p", c=16, p=128)
        m_mod_fin = [None] * 6
        last_g_evac = [None] * 6
        m_tr_prev = None
        for i in range(6):
            pbk = pb[i % 2]
            lw = [("ms", jt_dma[NJT2 - 1])]
            if m_tr_prev is not None:
                lw.append(("pe", m_tr_prev))
            m_modv = op("sync", lambda e, i=i: e.dma_start(
                out=modv1[:, :], in_=modscr_v[i]), waits=lw, inc="mv")
            wts = [("mv", m_modv), ("sm2", m_id)]
            if i >= 2:
                wts.append(m_mod_fin[i - 2])
            mtr = op("tensor", lambda e, i=i, pbk=pbk: e.transpose(
                pbk[:, 0:16], modv1[:, :], ident[0:16, 0:16]),
                waits=wts, inc="pe")
            m_tr_prev = mtr
            if i in (0, 3):
                mf = ("s", op("scalar", lambda e, i=i, pbk=pbk: e.copy(
                    mod_tiles[i][:, :], pbk[:, 0:16]), waits=[("pe", mtr)],
                    inc="s"))
            elif i in (1, 4):
                mf = ("s", op("scalar", lambda e, i=i, pbk=pbk: e.activation(
                    out=mod_tiles[i][:, :], in_=pbk[:, 0:16], func=AF.Identity,
                    bias=1.0, scale=1.0), waits=[("pe", mtr)], inc="s"))
            else:
                mf = ("s", op("scalar", lambda e, i=i, pbk=pbk: e.copy(
                    mod_tiles[i][:, :], pbk[:, 0:16]), waits=[("pe", mtr)],
                    inc="s"))
                last_g_evac[i] = mf[1]
            m_mod_fin[i] = mf

        for v in (2, 5):
            gtile = mod_tiles[v]
            m_v1 = op("vector", lambda e, gtile=gtile: e.tensor_scalar(
                g1_t[:, :], gtile[:, :], 0.0, None, ALU.is_ge),
                waits=[("s", last_g_evac[v])] + ([m_mod_fin[2]] if v == 5 else []),
                inc="v")
            m_s2 = op("scalar", lambda e: e.activation(
                out=g1_t[:, :], in_=g1_t[:, :], func=AF.Identity,
                scale=2.0, bias=n1_t[:, 0:1]),
                waits=[("v", m_v1), ("v", m_n1)], inc="s")
            m_s3 = op("scalar", lambda e, gtile=gtile: e.activation(
                out=g3_t[:, :], in_=gtile[:, :], func=AF.Abs), inc="s")
            m_v4 = op("vector", lambda e: e.tensor_tensor(
                g4_t[:, :], g3_t[:, :], q25_t[:, :], ALU.max),
                waits=[("s", m_s3)], inc="v")
            mfg = op("gpsimd", lambda e, gtile=gtile: e.tensor_tensor(
                gtile[:, :], g1_t[:, :], g4_t[:, :], ALU.mult),
                waits=[("s", m_s3), ("v", m_v4)], inc="gc")
            m_mod_fin[v] = ("gc", mfg)

        bank_free = [None] * 8
        bank_free[0] = m_mod_fin[4]
        bank_free[1] = m_mod_fin[5]
        bank_free[6] = jt_ev[NJT2 - 2]
        bank_free[7] = jt_ev[NJT2 - 1]

        slot_n = [0]
        pair_consumed = []
        state = {"h_s": 0, "h_v": 0, "vh_s": 0, "vh_v": 0,
                 "last_out_v": 0, "last_qkv_pe": 0, "last_wo_pe": 0,
                 "last_h2use_pe": 0, "last_lnfin_v": 0}

        used_slots = set()

        def load_pair(wn, mc, ring_eng="sync"):
            res = []
            marks = []
            for half in ("hi", "lo"):
                sidx = (2 * slot_n[0] + (0 if half == "hi" else 1)) % NSLOT
                wts = list(wsplit_done[wn])
                if sidx >= 5 and sidx not in used_slots:
                    wts += prep_clear
                used_slots.add(sidx)
                prev_pair = slot_n[0] - NSLOT // 2
                if prev_pair >= 0:
                    wts.append(("pe", pair_consumed[prev_pair]))
                m = op(ring_eng, lambda e, wn=wn, mc=mc, half=half, sidx=sidx:
                       e.dma_start(out=ring[sidx][:, :].rearrange(
                           "p (k j) -> p k j", k=NCH),
                           in_=wscr[wn][half][:, mc, :, :]),
                       waits=wts, inc=f"rs{sidx}")
                res.append(sidx)
                marks.append((f"rs{sidx}", m))
            slot_n[0] += 1
            return ring[res[0]], ring[res[1]], marks

        def mm_group(pbank_ap, slot_hi, slot_lo, ld_mark, mv_hi, mv_lo,
                     bank_idx, extra_waits=(), extra_mms=(), kc_waits=None):
            wts = list(ld_mark) + [bank_free[bank_idx]] + list(extra_waits)
            nmm = NCH * 3 + len(extra_mms)
            cnt = 0
            mark = None
            for kc in range(NCH):
                kw = list(kc_waits[kc]) if kc_waits is not None else []
                for (ws, mvv) in ((slot_hi, mv_hi), (slot_hi, mv_lo),
                                  (slot_lo, mv_hi)):
                    first = cnt == 0
                    last = cnt == nmm - 1
                    mark = op("tensor", lambda e, ws=ws, kc=kc, mvv=mvv,
                              first=first, last=last, pbank_ap=pbank_ap:
                              e.matmul(pbank_ap,
                                       ws[:, kc * 128:(kc + 1) * 128],
                                       chs(mvv, kc), start=first, stop=last),
                              waits=((wts + kw) if first else kw),
                              inc=("pe" if last else None))
                    kw = []
                    cnt += 1
            for emm in extra_mms:
                last = cnt == nmm - 1
                mark = op("tensor", lambda e, emm=emm, last=last,
                          pbank_ap=pbank_ap: emm(e, pbank_ap, last),
                          inc=("pe" if last else None))
                cnt += 1
            pair_consumed.append(mark)
            return mark

        def stats_phase(dep_wait, dep_dc=None):
            mark_sa = None
            for dc in range(NCH):
                wts = ([dep_wait, bank_free[0]] if dc == 0 else []) \
                    if dep_dc is None else \
                    (([bank_free[0]] if dc == 0 else []) + [dep_dc[dc]])
                mark_sa = op("tensor", lambda e, dc=dc: e.matmul(
                    pb[0][:, :], ones_t[:, :], chs(x_t, dc),
                    start=(dc == 0), stop=(dc == NCH - 1)),
                    waits=wts, inc=("pe" if dc == NCH - 1 else None))
            sq_mm = [None] * NCH
            mark_sb = None
            for dc in range(NCH):
                bi = dc % 2
                wts = [dep_dc[dc]] if dep_dc is not None else [dep_wait]
                if state["last_lnfin_v"]:
                    wts.append(("v", state["last_lnfin_v"]))
                if dc >= 2:
                    wts.append(("pe", sq_mm[dc - 2]))
                msq = op("scalar", lambda e, dc=dc, bi=bi: e.activation(
                    out=sq_t[:, bi * ST:(bi + 1) * ST], in_=chs(x_t, dc),
                    func=AF.Square), waits=wts, inc="s")
                wts2 = [("s", msq)]
                if dc == 0:
                    wts2.append(bank_free[1])
                mark_sb = op("tensor", lambda e, dc=dc, bi=bi: e.matmul(
                    pb[1][:, :], ones_t[:, :], sq_t[:, bi * ST:(bi + 1) * ST],
                    start=(dc == 0), stop=(dc == NCH - 1)),
                    waits=wts2, inc="pe")
                sq_mm[dc] = mark_sb
            fin_w = [("pe", mark_sb), ("pe", mark_sa)]
            if state["last_lnfin_v"]:
                fin_w.append(("v", state["last_lnfin_v"]))
            m_m2 = op("scalar", lambda e: e.activation(
                out=m2_t[:, :], in_=pb[0][:, :], func=AF.Square, scale=1.0 / D),
                waits=fin_w, inc="s")
            m_ve = op("scalar", lambda e: e.activation(
                out=ve_t[:, :], in_=pb[1][:, :], func=AF.Identity,
                bias=eps_t[:, 0:1], scale=1.0 / D),
                waits=[("v", m_eps)], inc="s")
            m_mu = op("scalar", lambda e: e.activation(
                out=mu_t[:, :], in_=pb[0][:, :], func=AF.Identity, scale=1.0 / D),
                inc="s")
            m_ve2 = op("vector", lambda e: e.tensor_tensor(
                ve_t[:, :], ve_t[:, :], m2_t[:, :], ALU.subtract),
                waits=[("s", m_ve)], inc="v")
            m_rc = op("vector", lambda e: e.reciprocal(r_t[:, :], ve_t[:, :]),
                      inc="v")
            m_r = op("scalar", lambda e: e.activation(
                out=r_t[:, :], in_=r_t[:, :], func=AF.Sqrt),
                waits=[("v", m_rc)], inc="s")
            op("vector", lambda e: e.tensor_tensor(
                mu_t[:, :], mu_t[:, :], r_t[:, :], ALU.mult),
                waits=[("s", m_r)], inc="v")
            bank_free[0] = ("s", m_mu)
            bank_free[1] = ("s", m_ve)
            return m_r

        def ln_apply(m_r, c1_tile, s_tile, dep_pe):
            prev_f = [None, None]
            prev_t1 = [None, None]
            mhi = mlo = None
            hdc = []
            for dc in range(NCH):
                bi = dc % 2
                w1 = []
                if dc == 0:
                    w1 = [("s", m_r)]
                if prev_t1[bi] is not None:
                    w1.append(("s", prev_t1[bi]))
                m_t1 = op("vector", lambda e, dc=dc, bi=bi: e.tensor_tensor(
                    t1_t[:, bi * ST:(bi + 1) * ST], chs(x_t, dc), r_t[:, :],
                    ALU.mult), waits=w1, inc="v")
                m_t2 = op("vector", lambda e, bi=bi: e.tensor_tensor(
                    t1_t[:, bi * ST:(bi + 1) * ST], t1_t[:, bi * ST:(bi + 1) * ST],
                    mu_t[:, :], ALU.subtract), inc="v")
                w2 = [("v", m_t2)]
                if dc == 0 and dep_pe:
                    w2.append(("pe", dep_pe))
                if prev_f[bi] is not None:
                    w2.append(("v", prev_f[bi]))
                m_f = op("scalar", lambda e, bi=bi, dc=dc, c1_tile=c1_tile,
                         s_tile=s_tile: e.activation(
                             out=h1f[:, bi * ST:(bi + 1) * ST],
                             in_=t1_t[:, bi * ST:(bi + 1) * ST],
                             func=AF.Identity, scale=c1_tile[:, dc:dc + 1],
                             bias=s_tile[:, dc:dc + 1]), waits=w2, inc="s")
                prev_t1[bi] = m_f
                mhi = op("scalar", lambda e, dc=dc, bi=bi: e.copy(
                    chs(h_hi, dc), h1f[:, bi * ST:(bi + 1) * ST]), inc="s")
                mlo = op("vector", lambda e, dc=dc, bi=bi: e.tensor_tensor(
                    chs(h_lo, dc), h1f[:, bi * ST:(bi + 1) * ST], chs(h_hi, dc),
                    ALU.subtract), waits=[("s", mhi)], inc="v")
                prev_f[bi] = mlo
                hdc.append([("s", mhi), ("v", mlo)])
            state["h_s"], state["h_v"] = mhi, mlo
            state["h_dc"] = hdc
            state["last_lnfin_v"] = mlo

        def dump(nm, buf, waits):
            if not debug:
                return
            op("gpsimd", lambda e, nm=nm, buf=buf: e.dma_start(
                out=dbg[nm], in_=buf), waits=waits, inc="gd")

        for t in range(nt):
            m_r = stats_phase(None, dep_dc=x_dc[t])
            ln_apply(m_r, c1a_t, sa_t, state["last_h2use_pe"])
            if t == 0:
                dump("d_mod", sa_t.tensor.ap()[:, 0:96] if False else None, None) if False else None
                if debug:
                    mm6 = [m for m in m_mod_fin]
                    op("gpsimd", lambda e: e.dma_start(
                        out=dbg["d_mod"],
                        in_=sa_t[:, :]) if False else e.dma_start(
                        out=dbg["d_mod"][:, 0:16], in_=sa_t[:, :]),
                        waits=[mm6[0]], inc="gd")
                    for ii, til in enumerate((c1a_t, ga_t, sb_t, c1b_t, gb_t)):
                        op("gpsimd", lambda e, ii=ii, til=til: e.dma_start(
                            out=dbg["d_mod"][:, (ii + 1) * 16:(ii + 2) * 16],
                            in_=til[:, :]), waits=[mm6[ii + 1]], inc="gd")
                    dump("d_r", r_t[:, :], [("s", m_r)])
                    dump("d_b", mu_t[:, :], [("v", state["last_lnfin_v"])])
                    dump("d_hhi", h_hi[:, :], [("s", state["h_s"]), ("v", state["h_v"])])
                    dump("d_hlo", h_lo[:, :], [("v", state["h_v"])])
            kevac_prev = [None, None]
            vhf_prev = [None, None]
            m_vhhi = m_vhlo = None
            vh_dc = []
            for ec in range(NCH):
                banks = (0, 1, 2) if ec % 2 == 0 else (3, 4, 5)
                marks = {}
                for pi, wn in enumerate(("q", "k", "v")):
                    sh, slo, mld = load_pair(wn, ec, "scalar" if t == 0 else "sync")
                    bidx = banks[pi]
                    marks[wn] = mm_group(pb[bidx][:, :], sh, slo, mld, h_hi,
                                         h_lo, bidx, kc_waits=state["h_dc"])
                bi = ec % 2
                wk_e = [("pe", marks["k"])]
                if kevac_prev[bi] is not None:
                    wk_e.append(("v", kevac_prev[bi]))
                m_ke = op("scalar", lambda e, bi=bi, b=banks[1]: e.copy(
                    ktmp[:, bi * ST:(bi + 1) * ST], pb[b][:, :]),
                    waits=wk_e, inc="s")
                m_kv = op("vector", lambda e, bi=bi, b=banks[2]: e.tensor_tensor(
                    kvf[:, :], ktmp[:, bi * ST:(bi + 1) * ST], pb[b][:, :],
                    ALU.mult), waits=[("s", m_ke), ("pe", marks["v"])], inc="v")
                kevac_prev[bi] = m_kv
                init = 0.0 if t == 0 else carry[:, ec:ec + 1]
                op("vector", lambda e, init=init: e.tensor_tensor_scan(
                    kvt[:, :], zeros_t[:, :], kvf[:, :], init, ALU.add, ALU.add),
                    inc="v")
                wv_h = [("pe", marks["q"])]
                if vhf_prev[bi] is not None:
                    wv_h.append(("s", vhf_prev[bi]))
                m_vf = op("vector", lambda e, bi=bi, b=banks[0]:
                          e.scalar_tensor_tensor(
                              out=vhf[:, bi * ST:(bi + 1) * ST], in0=kvt[:, :],
                              scalar=DENOM, in1=pb[b][:, :], op0=ALU.mult,
                              op1=ALU.mult), waits=wv_h, inc="v")
                whi_w = [("v", m_vf)]
                if ec == 0 and state["last_wo_pe"]:
                    whi_w.append(("pe", state["last_wo_pe"]))
                op("vector", lambda e, ec=ec: e.tensor_copy(
                    carry[:, ec:ec + 1], kvt[:, ST - 1:ST]), inc="v")
                m_vhhi = op("scalar", lambda e, ec=ec, bi=bi: e.copy(
                    chs(vh_hi, ec), vhf[:, bi * ST:(bi + 1) * ST]),
                    waits=whi_w, inc="s")
                vhf_prev[bi] = m_vhhi
                m_vhlo = op("vector", lambda e, ec=ec, bi=bi: e.tensor_tensor(
                    chs(vh_lo, ec), vhf[:, bi * ST:(bi + 1) * ST],
                    chs(vh_hi, ec), ALU.subtract), waits=[("s", m_vhhi)],
                    inc="v")
                vh_dc.append([("s", m_vhhi), ("v", m_vhlo)])
                for b in banks:
                    bank_free[b] = ("v", m_vf)
            state["last_qkv_pe"] = pair_consumed[-1]
            state["vh_s"], state["vh_v"] = m_vhhi, m_vhlo
            state["vh_dc"] = vh_dc
            if t == 0:
                dump("d_vhhi", vh_hi[:, :], [("s", m_vhhi), ("v", m_vhlo)])
                dump("d_vhlo", vh_lo[:, :], [("v", m_vhlo)])
            m_x2_last = None
            m_x2_dc = []
            for fc in range(NCH):
                bidx = 6 + fc % 2
                sh, slo, mld = load_pair("o", fc, "scalar" if t == 0 else "sync")
                mwo = mm_group(pb[bidx][:, :], sh, slo, mld, vh_hi, vh_lo,
                               bidx, kc_waits=state["vh_dc"])
                m_x2 = op("vector", lambda e, fc=fc, bidx=bidx:
                          e.scalar_tensor_tensor(
                              out=chs(x_t, fc), in0=pb[bidx][:, :],
                              scalar=ga_t[:, fc:fc + 1], in1=chs(x_t, fc),
                              op0=ALU.mult, op1=ALU.add),
                          waits=[("pe", mwo), m_mod_fin[2]], inc="v")
                m_x2_last = m_x2
                m_x2_dc.append(("v", m_x2))
                bank_free[bidx] = ("v", m_x2)
            state["last_wo_pe"] = pair_consumed[-1]
            if t == 0:
                dump("d_x2", x_t[:, :], [("v", m_x2_last)])
            m_r2 = stats_phase(("v", m_x2_last), dep_dc=m_x2_dc)
            ln_apply(m_r2, c1b_t, sb_t, state["last_qkv_pe"])
            if t == 0:
                dump("d_r2", r_t[:, :], [("s", m_r2)])
                dump("d_rthi", rt_hi[:, 0:NCH * 8], [("s", m_ab_s)])
                dump("d_rtlo", rt_lo[:, 0:NCH * 8], [("v", m_rtlo)])
                dump("d_h2hi", h_hi[:, :], [("s", state["h_s"]), ("v", state["h_v"])])
                dump("d_h2lo", h_lo[:, :], [("v", state["h_v"])])
            rmark = None
            for dc in range(NCH):
                for pi, (ws, mvv) in enumerate(((rt_hi, h_hi), (rt_hi, h_lo),
                                                (rt_lo, h_hi))):
                    first = dc == 0 and pi == 0
                    last = dc == NCH - 1 and pi == 2
                    wts = list(state["h_dc"][dc]) if pi == 0 else []
                    if first:
                        wts += [bank_free[2], ("v", m_rtlo)]
                    rmark = op("tensor", lambda e, dc=dc, ws=ws, mvv=mvv,
                               first=first, last=last: e.matmul(
                                   pb[2][:, :], ws[:, dc * 128:(dc + 1) * 128],
                                   chs(mvv, dc), start=first, stop=last),
                               waits=wts, inc=("pe" if last else None))
            m_lg = op("scalar", lambda e: e.copy(lg_t[:, :], pb[2][0:8, :]),
                      waits=[("pe", rmark)], inc="s")
            bank_free[2] = ("s", m_lg)
            if t == 0:
                dump("d_lg", lg_t[:, :], [("s", m_lg)])
            m_m8_last = None
            m_m8_prev = None
            for sl in range(4):
                bidx = 3 + sl % 2
                mtr = op("tensor", lambda e, sl=sl, bidx=bidx: e.transpose(
                    pb[bidx][:, 0:8], lg_t[:, sl * 128:(sl + 1) * 128],
                    ident[0:8, 0:8]),
                    waits=[("s", m_lg), bank_free[bidx]], inc="pe")
                m_lt = op("scalar", lambda e, sl=sl, bidx=bidx: e.copy(
                    lt_t[:, sl * 8:(sl + 1) * 8], pb[bidx][:, 0:8]),
                    waits=[("pe", mtr)], inc="s")
                bank_free[bidx] = ("s", m_lt)
                m_mx = op("vector", lambda e, sl=sl: e.reduce_max(
                    lm_t[:, sl:sl + 1], lt_t[:, sl * 8:(sl + 1) * 8],
                    mybir.AxisListType.X), waits=[("s", m_lt)], inc="v")
                m_df = op("scalar", lambda e, sl=sl: e.activation(
                    out=dt_t[:, sl * 8:(sl + 1) * 8],
                    in_=lt_t[:, sl * 8:(sl + 1) * 8], func=AF.Identity,
                    scale=sneg_t[:, 0:1], bias=lm_t[:, sl:sl + 1]),
                    waits=[("v", m_mx), ("v", m_sneg)], inc="s")
                m_mt = op("vector", lambda e, sl=sl: e.tensor_scalar(
                    mt_t[:, sl * 8:(sl + 1) * 8], dt_t[:, sl * 8:(sl + 1) * 8],
                    0.0, None, ALU.is_equal), waits=[("s", m_df)], inc="v")
                tb_w = [("v", m_mt)]
                if sl == 0:
                    tb_w.append(bank_free[5])
                else:
                    tb_w.append(("s", m_m8_prev))
                mtb = op("tensor", lambda e, sl=sl: e.transpose(
                    pb[5][0:8, 0:128], mt_t[:, sl * 8:(sl + 1) * 8],
                    ident[:, :]), waits=tb_w, inc="pe")
                m_m8 = op("scalar", lambda e, sl=sl: e.copy(
                    m8_t[:, sl * 128:(sl + 1) * 128], pb[5][0:8, 0:128]),
                    waits=[("pe", mtb)], inc="s")
                m_m8_prev = m_m8
                m_m8_last = m_m8
            bank_free[5] = ("s", m_m8_last)
            m_mex = op("tensor", lambda e: e.matmul(
                pb[6][:, :], emat[:, :], m8_t[:, :], start=True, stop=True),
                waits=[("s", m_m8_last), bank_free[6], ("sm3", m_e7)], inc="pe")
            m_mf = op("scalar", lambda e: e.copy(mf_t[:, :], pb[6][:, :]),
                      waits=[("pe", m_mex)], inc="s")
            bank_free[6] = ("s", m_mf)
            if t == 0:
                dump("d_mf", mf_t[:, :], [("s", m_mf)])
            amark = None
            for dc in range(NCH):
                for pi, (ws, mvv) in enumerate(((a_hi, h_hi), (a_hi, h_lo),
                                                (a_lo, h_hi))):
                    first = dc == 0 and pi == 0
                    last = dc == NCH - 1 and pi == 2
                    wts = list(state["h_dc"][dc]) if pi == 0 else []
                    if first:
                        wts += [bank_free[7], ("v", m_ab_v)]
                    amark = op("tensor", lambda e, dc=dc, ws=ws, mvv=mvv,
                               first=first, last=last: e.matmul(
                                   pb[7][:, :], ws[:, dc * 128:(dc + 1) * 128],
                                   chs(mvv, dc), start=first, stop=last),
                               waits=wts, inc=("pe" if last else None))
            m_am = op("vector", lambda e: e.tensor_tensor(
                amf[:, :], pb[7][:, :], mf_t[:, :], ALU.mult),
                waits=[("pe", amark), ("s", m_mf)], inc="v")
            bank_free[7] = ("v", m_am)
            if t == 0:
                dump("d_amf", amf[:, :], [("v", m_am)])
            m_amh = op("scalar", lambda e: e.copy(am_hi[:, :], amf[:, :]),
                       waits=[("v", m_am)], inc="s")
            m_aml = op("vector", lambda e: e.tensor_tensor(
                am_lo[:, :], amf[:, :], am_hi[:, :], ALU.subtract),
                waits=[("s", m_amh)], inc="v")
            tb_prev = [None, None]
            ot_prev = [None, None]
            for oc in range(NCH):
                bidx = oc % 3
                sh, slo, mld = load_pair("b", oc, "scalar" if t == 0 else "sync")

                def mk_extra(i, oc=oc):
                    def f(e, pbank_ap, last):
                        ws, mvv = ((bw_hi, am_hi), (bw_hi, am_lo),
                                   (bw_lo, am_hi))[i]
                        return e.matmul(pbank_ap,
                                        ws[:, oc * 128:(oc + 1) * 128],
                                        mvv[:, :], start=False, stop=last)
                    return f
                mb_ = mm_group(pb[bidx][:, :], sh, slo, mld, h_hi, h_lo, bidx,
                               extra_waits=[("s", m_amh), ("v", m_aml)],
                               extra_mms=[mk_extra(0), mk_extra(1),
                                          mk_extra(2)],
                               kc_waits=state["h_dc"])
                bi = oc % 2
                wtb = [("pe", mb_)]
                if tb_prev[bi] is not None:
                    wtb.append(("v", tb_prev[bi]))
                m_tb = op("scalar", lambda e, bi=bi, oc=oc, bidx=bidx:
                          e.activation(out=tb_t[:, bi * ST:(bi + 1) * ST],
                                       in_=pb[bidx][:, :], func=AF.Identity,
                                       bias=bb_t[:, oc:oc + 1], scale=1.0),
                          waits=wtb, inc="s")
                wot = [("s", m_tb)]
                if ot_prev[bi] is not None:
                    wot.append((f"os{bi}", ot_prev[bi]))
                m_ot = op("vector", lambda e, bi=bi, oc=oc:
                          e.scalar_tensor_tensor(
                              out=ot_t[:, bi * ST:(bi + 1) * ST],
                              in0=tb_t[:, bi * ST:(bi + 1) * ST],
                              scalar=gb_t[:, oc:oc + 1], in1=chs(x_t, oc),
                              op0=ALU.mult, op1=ALU.add),
                          waits=wot + [m_mod_fin[5]], inc="v")
                tb_prev[bi] = m_ot
                bank_free[bidx] = ("s", m_tb)
                m_og = op("scalar", lambda e, t=t, oc=oc, bi=bi: e.dma_start(
                    out=outTr[:, oc, t * ST:(t + 1) * ST],
                    in_=ot_t[:, bi * ST:(bi + 1) * ST]),
                    waits=[("v", m_ot)], inc=f"os{bi}")
                ot_prev[bi] = m_og
                state["last_out_v"] = m_ot
                if t + 1 < nt:
                    mx_ = op("scalar", lambda e, t=t, oc=oc: e.dma_start(
                        out=chs(x_t, oc), in_=xTr[:, oc, (t + 1) * ST:(t + 2) * ST]),
                        waits=[("v", m_ot)], inc=f"xc{oc}")
                    x_dc[t + 1][oc] = (f"xc{oc}", mx_)
            state["last_h2use_pe"] = pair_consumed[-1]

        final_gd = CNT["gd"]
        final_sd = CNT["sd"]
        final_os = (CNT["os0"], CNT["os1"])

        def emit(engname, engobj):
            memo = {k: 0 for k in SEM_ENG}
            for (eng, fn, waits, inc, amt) in OPS:
                if eng != engname:
                    continue
                for (sname, val) in waits:
                    if val > memo[sname]:
                        engobj.wait_ge(sems[sname], val)
                        memo[sname] = val
                ins = fn(engobj)
                if inc is not None:
                    ins.then_inc(sems[inc], amt)

        @block.sync
        def _(sync):
            emit("sync", sync)
            sync.wait_ge(sems["sd"], final_sd)
            sync.wait_ge(sems["os0"], final_os[0])
            sync.wait_ge(sems["os1"], final_os[1])

        @block.gpsimd
        def _(gpsimd):
            emit("gpsimd", gpsimd)
            gpsimd.wait_ge(sems["gd"], final_gd)

        @block.vector
        def _(vector):
            emit("vector", vector)

        @block.scalar
        def _(scalar):
            emit("scalar", scalar)

        @block.tensor
        def _(tensor):
            emit("tensor", tensor)

    return nc


_NC_CACHE = {}


def _get_nc():
    if "nc" not in _NC_CACHE:
        _NC_CACHE["nc"] = build_nc()
    return _NC_CACHE["nc"]


def make_in_maps(L0_attn_x, L0_attn_Wq, L0_attn_Wk, L0_attn_Wv, L0_attn_Wo,
                 L0_ffn_W_base, L0_ffn_b_base, L0_ffn_router_W, L0_ffn_A,
                 L0_ffn_Bw, L0_c, L0_wmod):
    B = L0_attn_x.shape[0]
    asc = np.ascontiguousarray
    wqT = asc(np.asarray(L0_attn_Wq).T)
    wkT = asc(np.asarray(L0_attn_Wk).T)
    wvT = asc(np.asarray(L0_attn_Wv).T)
    woT = asc(np.asarray(L0_attn_Wo).T)
    wbT = asc(np.asarray(L0_ffn_W_base).T)
    wmod_a = asc(np.asarray(L0_wmod))
    rts_np = np.zeros((128, NCH, 128), dtype=np.float32)
    rts_np[:, :, 0:8] = (np.asarray(L0_ffn_router_W).T.reshape(NCH, 128, 8)
                         .transpose(1, 0, 2))
    rts = asc(rts_np.reshape(128, NCH * 128))
    ats = asc(np.asarray(L0_ffn_A).reshape(128, D).T.reshape(NCH, 128, 128)
              .transpose(1, 0, 2).reshape(128, NCH * 128))
    bwts = asc(np.asarray(L0_ffn_Bw).transpose(0, 2, 1).reshape(128, D))
    bb = asc(np.asarray(L0_ffn_b_base).reshape(NCH, 128).T)
    cc = np.asarray(L0_c)
    id_np = np.eye(128, dtype=np.float32)
    em_np = (np.repeat(np.eye(8, dtype=np.float32), 16, axis=1) * 2.0
             ).astype(ml_dtypes.bfloat16)
    in_maps = []
    for b in range(B):
        in_maps.append({
            "xT": asc(np.asarray(L0_attn_x[b]).T),
            "wq": wqT, "wk": wkT, "wv": wvT, "wo": woT, "wb": wbT,
            "wmod": wmod_a, "rts": rts, "ats": ats, "bwts": bwts,
            "c_in": asc(cc[b].reshape(NCH, 128).T),
            "bb_in": bb, "id_in": id_np, "em_in": em_np,
        })
    return in_maps


def kernel(L0_attn_x, L0_attn_Wq, L0_attn_Wk, L0_attn_Wv, L0_attn_Wo,
           L0_ffn_W_base, L0_ffn_b_base, L0_ffn_router_W, L0_ffn_A, L0_ffn_Bw,
           L0_c, L0_wmod):
    in_maps = make_in_maps(L0_attn_x, L0_attn_Wq, L0_attn_Wk, L0_attn_Wv,
                           L0_attn_Wo, L0_ffn_W_base, L0_ffn_b_base,
                           L0_ffn_router_W, L0_ffn_A, L0_ffn_Bw, L0_c, L0_wmod)
    nc = _get_nc()
    res = run_bass_kernel_spmd(nc, in_maps, core_ids=list(range(len(in_maps))))
    out = np.stack([np.asarray(r["outT"]).T for r in res.results], axis=0)
    return out


# version 15
# speedup vs baseline: 1.0408x; 1.0408x over previous
import numpy as np
import ml_dtypes
from contextlib import ExitStack
import concourse.bass as bass
import concourse.mybir as mybir
from concourse.bass_utils import run_bass_kernel_spmd

F32 = mybir.dt.float32
BF16 = mybir.dt.bfloat16
AF = mybir.ActivationFunctionType
ALU = mybir.AluOpType

D = 2048
S = 4096
ST = 512
NT = S // ST
NCH = D // 128
NJT = 12288 // ST
DENOM = float(np.sqrt(2048.0) / 4096.0)
EPS = 1e-5
NSLOT = 8

SEM_ENG = {"ld": "sync", "gd": "gpsimd", "gc": "gpsimd", "sd": "scalar",
           "pe": "tensor", "v": "vector", "s": "scalar",
           "wh0": "gpsimd", "wh1": "gpsimd", "wl0": "gpsimd", "wl1": "gpsimd",
           "sm0": "sync", "sm1": "sync", "sm2": "sync", "sm3": "sync",
           "ms": "scalar", "mv": "sync",
           "wa0": "sync", "wa1": "sync", "wa2": "sync", "wa3": "sync",
           "wb0": "scalar", "wb1": "scalar", "wb2": "scalar", "wb3": "scalar",
           "wr0": "sync", "wr1": "scalar", "wr2": "sync", "wr3": "scalar",
           "os0": "scalar", "os1": "scalar"}
for _i in range(8):
    SEM_ENG[f"rs{_i}"] = "sync"
for _i in range(16):
    SEM_ENG[f"xc{_i}"] = "scalar"


def build_nc(nt=NT, debug=False):
    nc = bass.Bass("TRN2", target_bir_lowering=False)

    xT = nc.dram_tensor("xT", [D, S], F32, kind="ExternalInput").ap()
    wq = nc.dram_tensor("wq", [D, D], F32, kind="ExternalInput").ap()
    wk = nc.dram_tensor("wk", [D, D], F32, kind="ExternalInput").ap()
    wv = nc.dram_tensor("wv", [D, D], F32, kind="ExternalInput").ap()
    wo = nc.dram_tensor("wo", [D, D], F32, kind="ExternalInput").ap()
    wb = nc.dram_tensor("wb", [D, D], F32, kind="ExternalInput").ap()
    wmod = nc.dram_tensor("wmod", [D, 12288], F32, kind="ExternalInput").ap()
    rts = nc.dram_tensor("rts", [128, NCH * 128], F32, kind="ExternalInput").ap()
    ats = nc.dram_tensor("ats", [128, NCH * 128], F32, kind="ExternalInput").ap()
    bwts = nc.dram_tensor("bwts", [128, NCH * 128], F32, kind="ExternalInput").ap()
    c_in = nc.dram_tensor("c_in", [128, NCH], F32, kind="ExternalInput").ap()
    id_in = nc.dram_tensor("id_in", [128, 128], F32, kind="ExternalInput").ap()
    em_in = nc.dram_tensor("em_in", [8, 128], BF16, kind="ExternalInput").ap()
    bb_in = nc.dram_tensor("bb_in", [128, NCH], F32, kind="ExternalInput").ap()
    outT = nc.dram_tensor("outT", [D, S], F32, kind="ExternalOutput").ap()
    modscr = nc.dram_tensor("modscr", [12288], F32).ap()
    dbg = {}
    if debug:
        for nm, shape, dty in (("d_mod", [128, 96], F32), ("d_r", [128, ST], F32),
                               ("d_b", [128, ST], F32), ("d_hhi", [128, NCH * ST], BF16),
                               ("d_hlo", [128, NCH * ST], BF16),
                               ("d_vhhi", [128, NCH * ST], BF16),
                               ("d_vhlo", [128, NCH * ST], BF16),
                               ("d_x2", [128, NCH * ST], F32),
                               ("d_lg", [8, ST], F32), ("d_mf", [128, ST], BF16),
                               ("d_h2hi", [128, NCH * ST], BF16),
                               ("d_h2lo", [128, NCH * ST], BF16),
                               ("d_r2", [128, ST], F32),
                               ("d_rthi", [128, NCH * 8], BF16),
                               ("d_rtlo", [128, NCH * 8], BF16),
                               ("d_amf", [128, ST], F32)):
            dbg[nm] = nc.dram_tensor(nm, shape, dty, kind="ExternalOutput").ap()
    wnames = ["q", "k", "v", "o", "b"]
    wsrc = {"q": wq, "k": wk, "v": wv, "o": wo, "b": wb}
    wscr = {}
    for wn in wnames:
        wscr[wn] = {
            "hi": nc.dram_tensor(f"w{wn}_hi", [128, NCH, NCH, 128], BF16).ap(),
            "lo": nc.dram_tensor(f"w{wn}_lo", [128, NCH, NCH, 128], BF16).ap(),
        }

    with ExitStack() as ctx:
        def sbt(name, shape, dtype):
            return ctx.enter_context(nc.sbuf_tensor(name, shape, dtype))

        x_t = sbt("x_t", [128, NCH * ST], F32)
        h_hi = sbt("h_hi", [128, NCH * ST], BF16)
        h_lo = sbt("h_lo", [128, NCH * ST], BF16)
        vh_hi = sbt("vh_hi", [128, NCH * ST], BF16)
        vh_lo = sbt("vh_lo", [128, NCH * ST], BF16)
        ring = [sbt(f"ring{i}", [128, NCH * 128], BF16) for i in range(NSLOT)]

        a_hi = sbt("a_hi", [128, NCH * 128], BF16)
        a_lo = sbt("a_lo", [128, NCH * 128], BF16)
        bw_hi = sbt("bw_hi", [128, NCH * 128], BF16)
        bw_lo = sbt("bw_lo", [128, NCH * 128], BF16)
        rt_hi = sbt("rt_hi", [128, NCH * 128], BF16)
        rt_lo = sbt("rt_lo", [128, NCH * 128], BF16)
        c_t = sbt("c_t", [128, NCH], F32)
        bb_t = sbt("bb_t", [128, NCH], F32)
        s_t = sbt("s_t", [128, NCH], F32)
        sa_t = sbt("sa_t", [128, NCH], F32)
        c1a_t = sbt("c1a_t", [128, NCH], F32)
        ga_t = sbt("ga_t", [128, NCH], F32)
        sb_t = sbt("sb_t", [128, NCH], F32)
        c1b_t = sbt("c1b_t", [128, NCH], F32)
        gb_t = sbt("gb_t", [128, NCH], F32)
        g1_t = sbt("g1_t", [128, NCH], F32)
        g2_t = sbt("g2_t", [128, NCH], F32)
        lg_t = sbt("lg_t", [8, ST], F32)
        modv1 = sbt("modv1", [16, 128], F32)
        modsb = lg_t
        lt_t = sbt("lt_t", [128, 4 * 8], F32)
        lm_t = sbt("lm_t", [128, 4], F32)
        mt_t = sbt("mt_t", [128, 4 * 8], F32)
        m8_t = sbt("m8_t", [8, ST], BF16)
        mf_t = sbt("mf_t", [128, ST], BF16)
        emat = sbt("emat", [8, 128], BF16)
        ident = sbt("ident", [128, 128], F32)
        ones_t = sbt("ones_t", [128, 128], F32)
        zeros_t = sbt("zeros_t", [128, ST], F32)
        eps_t = sbt("eps_t", [128, 1], F32)
        q25_t = sbt("q25_t", [128, NCH], F32)
        n1_t = sbt("n1_t", [128, 1], F32)
        sneg_t = sbt("sneg_t", [128, 1], F32)
        g3_t = sbt("g3_t", [128, NCH], F32)
        g4_t = sbt("g4_t", [128, NCH], F32)
        dt_t = sbt("dt_t", [128, 4 * 8], F32)
        m2_t = sbt("m2_t", [128, ST], F32)
        ve_t = sbt("ve_t", [128, ST], F32)
        r_t = sbt("r_t", [128, ST], F32)
        mu_t = sbt("mu_t", [128, ST], F32)
        t1_t = sbt("t1_t", [128, 2 * ST], F32)
        h1f = sbt("h1f", [128, 2 * ST], F32)
        sq_t = h1f
        ktmp = sbt("ktmp", [128, 2 * ST], F32)
        kvf = sbt("kvf", [128, ST], F32)
        kvt = sbt("kvt", [128, ST], F32)
        vhf = sbt("vhf", [128, 2 * ST], F32)
        carry = sbt("carry", [128, NCH], F32)
        amf = sbt("amf", [128, ST], F32)
        am_hi = sbt("am_hi", [128, ST], BF16)
        am_lo = sbt("am_lo", [128, ST], BF16)
        tb_t = sbt("tb_t", [128, 2 * ST], F32)
        ot_t = sbt("ot_t", [128, 2 * ST], F32)
        _otb = ot_t[:, :].bitcast(BF16)
        whit = _otb[:, 0:1024]
        wlot = _otb[:, 1024:2048]
        _r5f = ring[5][:, :].bitcast(F32)
        _r6f = ring[6][:, :].bitcast(F32)
        _r7f = ring[7][:, :].bitcast(F32)
        wm_slots = {
            0: [ktmp[:, 0:512], ktmp[:, 512:1024],
                _r6f[:, 0:512], _r6f[:, 512:1024]],
            1: [vhf[:, 0:512], vhf[:, 512:1024],
                _r7f[:, 0:512], _r7f[:, 512:1024]],
        }
        wtmp_h = [tb_t[:, 0:512], tb_t[:, 512:1024],
                  _r5f[:, 0:512], _r5f[:, 512:1024]]

        pb = [ctx.enter_context(nc.psum_tensor(f"pb{i}", [128, ST], F32))
              for i in range(8)]

        import os as _os
        if _os.environ.get("SBUF_REPORT"):
            print("sbuf remaining:", nc.sbuf_bytes_remaining)
        sems = {k: ctx.enter_context(nc.semaphore(f"sem_{k}")) for k in SEM_ENG}
        block = ctx.enter_context(nc.Block())

        OPS = []
        CNT = {k: 0 for k in SEM_ENG}

        def op(eng, fn, waits=(), inc=None, amt=None, dma_op=False):
            ws = []
            for w in waits:
                if w is None:
                    continue
                sname, val = w
                if val is None or val <= 0:
                    continue
                ws.append((sname, val))
            mark = None
            if inc is not None:
                if amt is None:
                    amt = 1 if inc in ("gc", "pe", "v", "s") else 16
                CNT[inc] += amt
                mark = CNT[inc]
            OPS.append((eng, fn, ws, inc, amt))
            return mark

        def chs(buf, c, width=ST):
            return buf[:, c * width:(c + 1) * width]

        m_c = op("sync", lambda e: e.dma_start(out=c_t[:, :], in_=c_in), inc="sm0")
        m_bb = op("sync", lambda e: e.dma_start(out=bb_t[:, :], in_=bb_in), inc="sm1")
        m_id = op("sync", lambda e: e.dma_start(out=ident[:, :], in_=id_in), inc="sm2")
        m_e7 = op("sync", lambda e: e.dma_start(out=emat[:, :], in_=em_in), inc="sm3")
        op("vector", lambda e: e.memset(ones_t[:, :], 1.0), inc="v")
        op("vector", lambda e: e.memset(zeros_t[:, :], 0.0), inc="v")
        m_eps = op("vector", lambda e: e.memset(eps_t[:, :], EPS), inc="v")
        op("vector", lambda e: e.memset(q25_t[:, :], 0.25), inc="v")
        m_n1 = op("vector", lambda e: e.memset(n1_t[:, :], -1.0), inc="v")
        m_sneg = op("vector", lambda e: e.memset(sneg_t[:, :], -1.0), inc="v")
        m_silu = op("scalar", lambda e: e.activation(out=s_t[:, :], in_=c_t[:, :],
                    func=AF.Silu), waits=[("sm0", m_c)], inc="s")
        op("scalar", lambda e: e.copy(bb_t[:, 0:1], bb_t[:, 0:1]),
           waits=[("sm1", m_bb)], inc="s")

        xTr = xT.rearrange("(c p) s -> p c s", p=128)
        outTr = outT.rearrange("(c p) s -> p c s", p=128)
        x_dc = [[None] * NCH for _ in range(nt)]
        for dc in range(NCH):
            m = op("scalar", lambda e, dc=dc: e.dma_start(
                out=chs(x_t, dc), in_=xTr[:, dc, 0:ST]), inc=f"xc{dc}")
            x_dc[0][dc] = (f"xc{dc}", m)

        ab_hist = []
        for nm, src, hib, lob in (("a", ats, a_hi, a_lo), ("bw", bwts, bw_hi, bw_lo),
                                  ("rt", rts, rt_hi, rt_lo)):
            for hr in range(4):
                g = len(ab_hist)
                sl = slice(hr * 512, (hr + 1) * 512)
                bi = g % 4
                wtv = wtmp_h[bi]
                qeng = "sync" if bi % 2 == 0 else "scalar"
                lw = []
                if g >= 4:
                    lw = [("s", ab_hist[g - 4][0]), ("v", ab_hist[g - 4][1])]
                mload = op(qeng, lambda e, sl=sl, wtv=wtv, src=src: e.dma_start(
                    out=wtv, in_=src[:, sl]), waits=lw, inc=f"wr{bi}")
                mhi = op("scalar", lambda e, sl=sl, wtv=wtv, hib=hib: e.copy(
                    hib[:, sl], wtv), waits=[(f"wr{bi}", mload)], inc="s")
                mlo = op("vector", lambda e, sl=sl, wtv=wtv, hib=hib, lob=lob:
                         e.tensor_tensor(lob[:, sl], wtv, hib[:, sl],
                                         ALU.subtract),
                    waits=[("s", mhi)], inc="v")
                ab_hist.append((mhi, mlo))
        m_ab_s, m_ab_v = ab_hist[-1]
        m_rtlo = m_ab_v

        NJT2 = 24
        mod_tiles = [sa_t, c1a_t, ga_t, sb_t, c1b_t, gb_t]
        jt_ev = {}
        jt_dma = {}
        chain_cnt = [0, 0]
        chain_last_mm = {0: {}, 1: {}}
        prev_mod_dma = [None]

        def emit_jt(jt):
            chain = jt % 2
            qeng = "sync" if chain == 0 else "scalar"
            mm = None
            for dc in range(NCH):
                cj = chain_cnt[chain]
                chain_cnt[chain] += 1
                cslot = cj % 4
                buf = wm_slots[chain][cslot]
                lw = []
                if cj >= 4:
                    lw.append(("pe", chain_last_mm[chain][cj - 4]))
                wmsem = f"w{'a' if chain == 0 else 'b'}{cslot}"
                mld = op(qeng, lambda e, jt=jt, dc=dc, buf=buf: e.dma_start(
                    out=buf,
                    in_=wmod[dc * 128:(dc + 1) * 128, jt * 512:(jt + 1) * 512]),
                    waits=lw, inc=wmsem)
                wts = [(wmsem, mld), ("s", m_silu)]
                if dc == 0 and jt >= 2:
                    wts.append(jt_ev[jt - 2])
                mm = op("tensor", lambda e, jt=jt, dc=dc, buf=buf: e.matmul(
                    pb[6 + jt % 2][0:1, :], s_t[:, dc:dc + 1], buf,
                    start=(dc == 0), stop=(dc == NCH - 1)),
                    waits=wts, inc="pe")
                chain_last_mm[chain][cj] = mm
            wts = [("pe", mm)]
            if prev_mod_dma[0] is not None:
                wts.append(("ms", prev_mod_dma[0]))
            mev = op("scalar", lambda e, jt=jt: e.copy(
                modsb[0:1, :], pb[6 + jt % 2][0:1, :]), waits=wts, inc="s")
            jt_ev[jt] = ("s", mev)
            mdma = op("scalar", lambda e, jt=jt: e.dma_start(
                out=modscr[jt * ST:(jt + 1) * ST], in_=modsb[0:1, :]),
                waits=[("s", mev)], inc="ms", dma_op=True)
            prev_mod_dma[0] = mdma
            jt_dma[jt] = mdma

        hr_marks = []
        wsplit_done = {}

        def emit_unit(u):
            wn, kc, hr = u
            n_hr = len(hr_marks)
            bi = n_hr % 4
            wtv = wtmp_h[bi]
            sl = slice((n_hr % 2) * 512, (n_hr % 2 + 1) * 512)
            qeng = "sync" if bi % 2 == 0 else "scalar"
            wts = []
            if n_hr >= 4:
                wts += [("s", hr_marks[n_hr - 4][0]),
                        ("v", hr_marks[n_hr - 4][1])]
            else:
                wts += [("s", m_ab_s), ("v", m_ab_v)]
            mload = op(qeng, lambda e, wn=wn, kc=kc, hr=hr, wtv=wtv: e.dma_start(
                out=wtv,
                in_=wsrc[wn][kc * 128:(kc + 1) * 128, hr * 512:(hr + 1) * 512]),
                waits=wts, inc=f"wr{bi}")
            wts_h = [(f"wr{bi}", mload)]
            if n_hr >= 2:
                wts_h.append(hr_marks[n_hr - 2][2])
                wts_h.append(("v", hr_marks[n_hr - 2][1]))
            mhi = op("scalar", lambda e, sl=sl, wtv=wtv: e.copy(
                whit[:, sl], wtv), waits=wts_h, inc="s")
            wts_l = [("s", mhi)]
            if n_hr >= 2:
                wts_l.append(hr_marks[n_hr - 2][3])
            mlo = op("vector", lambda e, sl=sl, wtv=wtv: e.tensor_tensor(
                wlot[:, sl], wtv, whit[:, sl], ALU.subtract),
                waits=wts_l, inc="v")
            mcs = slice(hr * 4, (hr + 1) * 4)
            hb = n_hr % 2
            mwr_hi = op("gpsimd", lambda e, wn=wn, kc=kc, mcs=mcs, sl=sl:
                        e.dma_start(out=wscr[wn]["hi"][:, mcs, kc, :],
                        in_=whit[:, sl].rearrange("p (m j) -> p m j", m=4)),
                        waits=[("s", mhi)], inc=f"wh{hb}")
            mwr_lo = op("gpsimd", lambda e, wn=wn, kc=kc, mcs=mcs, sl=sl:
                        e.dma_start(out=wscr[wn]["lo"][:, mcs, kc, :],
                        in_=wlot[:, sl].rearrange("p (m j) -> p m j", m=4)),
                        waits=[("v", mlo)], inc=f"wl{hb}")
            hr_marks.append((mhi, mlo, (f"wh{hb}", mwr_hi), (f"wl{hb}", mwr_lo)))
            if (kc, hr) == (NCH - 1, 3):
                wsplit_done[wn] = [("wh0", CNT["wh0"]), ("wh1", CNT["wh1"]),
                                   ("wl0", CNT["wl0"]), ("wl1", CNT["wl1"])]

        units = [(wn, kc, hr) for wn in wnames for kc in range(NCH)
                 for hr in range(4)]
        INTERLEAVE = True
        u_i = 0
        for jt in range(NJT2):
            emit_jt(jt)
            tgt = (jt + 1) * len(units) // NJT2
            while u_i < tgt:
                emit_unit(units[u_i])
                u_i += 1

        prep_clear = [("s", hr_marks[-1][0]), ("v", hr_marks[-1][1]),
                      ("s", hr_marks[-2][0]), ("v", hr_marks[-2][1]),
                      ("pe", chain_last_mm[0][chain_cnt[0] - 1]),
                      ("pe", chain_last_mm[1][chain_cnt[1] - 1])]

        modscr_v = modscr.rearrange("(v c p) -> v c p", c=16, p=128)
        m_mod_fin = [None] * 6
        last_g_evac = [None] * 6
        m_tr_prev = None
        for i in range(6):
            pbk = pb[i % 2]
            lw = [("ms", jt_dma[NJT2 - 1])]
            if m_tr_prev is not None:
                lw.append(("pe", m_tr_prev))
            m_modv = op("sync", lambda e, i=i: e.dma_start(
                out=modv1[:, :], in_=modscr_v[i]), waits=lw, inc="mv")
            wts = [("mv", m_modv), ("sm2", m_id)]
            if i >= 2:
                wts.append(m_mod_fin[i - 2])
            mtr = op("tensor", lambda e, i=i, pbk=pbk: e.transpose(
                pbk[:, 0:16], modv1[:, :], ident[0:16, 0:16]),
                waits=wts, inc="pe")
            m_tr_prev = mtr
            if i in (0, 3):
                mf = ("s", op("scalar", lambda e, i=i, pbk=pbk: e.copy(
                    mod_tiles[i][:, :], pbk[:, 0:16]), waits=[("pe", mtr)],
                    inc="s"))
            elif i in (1, 4):
                mf = ("s", op("scalar", lambda e, i=i, pbk=pbk: e.activation(
                    out=mod_tiles[i][:, :], in_=pbk[:, 0:16], func=AF.Identity,
                    bias=1.0, scale=1.0), waits=[("pe", mtr)], inc="s"))
            else:
                mf = ("s", op("scalar", lambda e, i=i, pbk=pbk: e.copy(
                    mod_tiles[i][:, :], pbk[:, 0:16]), waits=[("pe", mtr)],
                    inc="s"))
                last_g_evac[i] = mf[1]
            m_mod_fin[i] = mf

        for v in (2, 5):
            gtile = mod_tiles[v]
            m_v1 = op("vector", lambda e, gtile=gtile: e.tensor_scalar(
                g1_t[:, :], gtile[:, :], 0.0, None, ALU.is_ge),
                waits=[("s", last_g_evac[v])] + ([m_mod_fin[2]] if v == 5 else []),
                inc="v")
            m_s2 = op("scalar", lambda e: e.activation(
                out=g1_t[:, :], in_=g1_t[:, :], func=AF.Identity,
                scale=2.0, bias=n1_t[:, 0:1]),
                waits=[("v", m_v1), ("v", m_n1)], inc="s")
            m_s3 = op("scalar", lambda e, gtile=gtile: e.activation(
                out=g3_t[:, :], in_=gtile[:, :], func=AF.Abs),
                waits=[("s", last_g_evac[v]), ("s", m_s2)], inc="s")
            m_v4 = op("vector", lambda e: e.tensor_tensor(
                g4_t[:, :], g3_t[:, :], q25_t[:, :], ALU.max),
                waits=[("s", m_s3)], inc="v")
            mfg = op("gpsimd", lambda e, gtile=gtile: e.tensor_tensor(
                gtile[:, :], g1_t[:, :], g4_t[:, :], ALU.mult),
                waits=[("s", m_s3), ("v", m_v4)], inc="gc")
            m_mod_fin[v] = ("gc", mfg)

        bank_free = [None] * 8
        bank_free[0] = m_mod_fin[4]
        bank_free[1] = m_mod_fin[5]
        bank_free[6] = jt_ev[NJT2 - 2]
        bank_free[7] = jt_ev[NJT2 - 1]

        slot_n = [0]
        pair_consumed = []
        state = {"h_s": 0, "h_v": 0, "vh_s": 0, "vh_v": 0,
                 "last_out_v": 0, "last_qkv_pe": 0, "last_wo_pe": 0,
                 "last_h2use_pe": 0, "last_lnfin_v": 0}

        used_slots = set()

        def load_pair(wn, mc, ring_eng="sync"):
            res = []
            marks = []
            for half in ("hi", "lo"):
                sidx = (2 * slot_n[0] + (0 if half == "hi" else 1)) % NSLOT
                wts = list(wsplit_done[wn])
                if sidx >= 5 and sidx not in used_slots:
                    wts += prep_clear
                used_slots.add(sidx)
                prev_pair = slot_n[0] - NSLOT // 2
                if prev_pair >= 0:
                    wts.append(("pe", pair_consumed[prev_pair]))
                m = op(ring_eng, lambda e, wn=wn, mc=mc, half=half, sidx=sidx:
                       e.dma_start(out=ring[sidx][:, :].rearrange(
                           "p (k j) -> p k j", k=NCH),
                           in_=wscr[wn][half][:, mc, :, :]),
                       waits=wts, inc=f"rs{sidx}")
                res.append(sidx)
                marks.append((f"rs{sidx}", m))
            slot_n[0] += 1
            return ring[res[0]], ring[res[1]], marks

        def mm_group(pbank_ap, slot_hi, slot_lo, ld_mark, mv_hi, mv_lo,
                     bank_idx, extra_waits=(), extra_mms=(), kc_waits=None):
            wts = list(ld_mark) + [bank_free[bank_idx]] + list(extra_waits)
            nmm = NCH * 3 + len(extra_mms)
            cnt = 0
            mark = None
            for kc in range(NCH):
                kw = list(kc_waits[kc]) if kc_waits is not None else []
                for (ws, mvv) in ((slot_hi, mv_hi), (slot_hi, mv_lo),
                                  (slot_lo, mv_hi)):
                    first = cnt == 0
                    last = cnt == nmm - 1
                    mark = op("tensor", lambda e, ws=ws, kc=kc, mvv=mvv,
                              first=first, last=last, pbank_ap=pbank_ap:
                              e.matmul(pbank_ap,
                                       ws[:, kc * 128:(kc + 1) * 128],
                                       chs(mvv, kc), start=first, stop=last),
                              waits=((wts + kw) if first else kw),
                              inc=("pe" if last else None))
                    kw = []
                    cnt += 1
            for emm in extra_mms:
                last = cnt == nmm - 1
                mark = op("tensor", lambda e, emm=emm, last=last,
                          pbank_ap=pbank_ap: emm(e, pbank_ap, last),
                          inc=("pe" if last else None))
                cnt += 1
            pair_consumed.append(mark)
            return mark

        def stats_phase(dep_wait, dep_dc=None):
            mark_sa = None
            for dc in range(NCH):
                wts = ([dep_wait, bank_free[0]] if dc == 0 else []) \
                    if dep_dc is None else \
                    (([bank_free[0]] if dc == 0 else []) + [dep_dc[dc]])
                mark_sa = op("tensor", lambda e, dc=dc: e.matmul(
                    pb[0][:, :], ones_t[:, :], chs(x_t, dc),
                    start=(dc == 0), stop=(dc == NCH - 1)),
                    waits=wts, inc=("pe" if dc == NCH - 1 else None))
            sq_mm = [None] * NCH
            mark_sb = None
            for dc in range(NCH):
                bi = dc % 2
                wts = [dep_dc[dc]] if dep_dc is not None else [dep_wait]
                if state["last_lnfin_v"]:
                    wts.append(("v", state["last_lnfin_v"]))
                if dc >= 2:
                    wts.append(("pe", sq_mm[dc - 2]))
                msq = op("scalar", lambda e, dc=dc, bi=bi: e.activation(
                    out=sq_t[:, bi * ST:(bi + 1) * ST], in_=chs(x_t, dc),
                    func=AF.Square), waits=wts, inc="s")
                wts2 = [("s", msq)]
                if dc == 0:
                    wts2.append(bank_free[1])
                mark_sb = op("tensor", lambda e, dc=dc, bi=bi: e.matmul(
                    pb[1][:, :], ones_t[:, :], sq_t[:, bi * ST:(bi + 1) * ST],
                    start=(dc == 0), stop=(dc == NCH - 1)),
                    waits=wts2, inc="pe")
                sq_mm[dc] = mark_sb
            fin_w = [("pe", mark_sb), ("pe", mark_sa)]
            if state["last_lnfin_v"]:
                fin_w.append(("v", state["last_lnfin_v"]))
            m_m2 = op("scalar", lambda e: e.activation(
                out=m2_t[:, :], in_=pb[0][:, :], func=AF.Square, scale=1.0 / D),
                waits=fin_w, inc="s")
            m_ve = op("scalar", lambda e: e.activation(
                out=ve_t[:, :], in_=pb[1][:, :], func=AF.Identity,
                bias=eps_t[:, 0:1], scale=1.0 / D),
                waits=[("v", m_eps)], inc="s")
            m_mu = op("scalar", lambda e: e.activation(
                out=mu_t[:, :], in_=pb[0][:, :], func=AF.Identity, scale=1.0 / D),
                inc="s")
            m_ve2 = op("vector", lambda e: e.tensor_tensor(
                ve_t[:, :], ve_t[:, :], m2_t[:, :], ALU.subtract),
                waits=[("s", m_ve)], inc="v")
            m_rc = op("vector", lambda e: e.reciprocal(r_t[:, :], ve_t[:, :]),
                      waits=[("v", m_ve2)], inc="v")
            m_r = op("scalar", lambda e: e.activation(
                out=r_t[:, :], in_=r_t[:, :], func=AF.Sqrt),
                waits=[("v", m_rc)], inc="s")
            op("vector", lambda e: e.tensor_tensor(
                mu_t[:, :], mu_t[:, :], r_t[:, :], ALU.mult),
                waits=[("s", m_r)], inc="v")
            bank_free[0] = ("s", m_mu)
            bank_free[1] = ("s", m_ve)
            return m_r

        def ln_apply(m_r, c1_tile, s_tile, dep_pe):
            prev_f = [None, None]
            prev_t1 = [None, None]
            mhi = mlo = None
            hdc = []
            for dc in range(NCH):
                bi = dc % 2
                w1 = []
                if dc == 0:
                    w1 = [("s", m_r)]
                if prev_t1[bi] is not None:
                    w1.append(("s", prev_t1[bi]))
                m_t1 = op("vector", lambda e, dc=dc, bi=bi: e.tensor_tensor(
                    t1_t[:, bi * ST:(bi + 1) * ST], chs(x_t, dc), r_t[:, :],
                    ALU.mult), waits=w1, inc="v")
                m_t2 = op("vector", lambda e, bi=bi: e.tensor_tensor(
                    t1_t[:, bi * ST:(bi + 1) * ST], t1_t[:, bi * ST:(bi + 1) * ST],
                    mu_t[:, :], ALU.subtract), waits=[("v", m_t1)], inc="v")
                w2 = [("v", m_t2)]
                if dc == 0 and dep_pe:
                    w2.append(("pe", dep_pe))
                if prev_f[bi] is not None:
                    w2.append(("v", prev_f[bi]))
                m_f = op("scalar", lambda e, bi=bi, dc=dc, c1_tile=c1_tile,
                         s_tile=s_tile: e.activation(
                             out=h1f[:, bi * ST:(bi + 1) * ST],
                             in_=t1_t[:, bi * ST:(bi + 1) * ST],
                             func=AF.Identity, scale=c1_tile[:, dc:dc + 1],
                             bias=s_tile[:, dc:dc + 1]), waits=w2, inc="s")
                prev_t1[bi] = m_f
                mhi = op("scalar", lambda e, dc=dc, bi=bi: e.copy(
                    chs(h_hi, dc), h1f[:, bi * ST:(bi + 1) * ST]),
                    waits=[("s", m_f)], inc="s")
                mlo = op("vector", lambda e, dc=dc, bi=bi: e.tensor_tensor(
                    chs(h_lo, dc), h1f[:, bi * ST:(bi + 1) * ST], chs(h_hi, dc),
                    ALU.subtract), waits=[("s", mhi)], inc="v")
                prev_f[bi] = mlo
                hdc.append([("s", mhi), ("v", mlo)])
            state["h_s"], state["h_v"] = mhi, mlo
            state["h_dc"] = hdc
            state["last_lnfin_v"] = mlo

        def dump(nm, buf, waits):
            if not debug:
                return
            op("gpsimd", lambda e, nm=nm, buf=buf: e.dma_start(
                out=dbg[nm], in_=buf), waits=waits, inc="gd")

        for t in range(nt):
            m_r = stats_phase(None, dep_dc=x_dc[t])
            ln_apply(m_r, c1a_t, sa_t, state["last_h2use_pe"])
            if t == 0:
                dump("d_mod", sa_t.tensor.ap()[:, 0:96] if False else None, None) if False else None
                if debug:
                    mm6 = [m for m in m_mod_fin]
                    op("gpsimd", lambda e: e.dma_start(
                        out=dbg["d_mod"],
                        in_=sa_t[:, :]) if False else e.dma_start(
                        out=dbg["d_mod"][:, 0:16], in_=sa_t[:, :]),
                        waits=[mm6[0]], inc="gd")
                    for ii, til in enumerate((c1a_t, ga_t, sb_t, c1b_t, gb_t)):
                        op("gpsimd", lambda e, ii=ii, til=til: e.dma_start(
                            out=dbg["d_mod"][:, (ii + 1) * 16:(ii + 2) * 16],
                            in_=til[:, :]), waits=[mm6[ii + 1]], inc="gd")
                    dump("d_r", r_t[:, :], [("s", m_r)])
                    dump("d_b", mu_t[:, :], [("v", state["last_lnfin_v"])])
                    dump("d_hhi", h_hi[:, :], [("s", state["h_s"]), ("v", state["h_v"])])
                    dump("d_hlo", h_lo[:, :], [("v", state["h_v"])])
            kevac_prev = [None, None]
            vhf_prev = [None, None]
            m_vhhi = m_vhlo = None
            vh_dc = []
            for ec in range(NCH):
                banks = (0, 1, 2) if ec % 2 == 0 else (3, 4, 5)
                marks = {}
                for pi, wn in enumerate(("q", "k", "v")):
                    sh, slo, mld = load_pair(wn, ec, "scalar" if t == 0 else "sync")
                    bidx = banks[pi]
                    marks[wn] = mm_group(pb[bidx][:, :], sh, slo, mld, h_hi,
                                         h_lo, bidx, kc_waits=state["h_dc"])
                bi = ec % 2
                wk_e = [("pe", marks["k"])]
                if kevac_prev[bi] is not None:
                    wk_e.append(("v", kevac_prev[bi]))
                m_ke = op("scalar", lambda e, bi=bi, b=banks[1]: e.copy(
                    ktmp[:, bi * ST:(bi + 1) * ST], pb[b][:, :]),
                    waits=wk_e, inc="s")
                kv_w = [("s", m_ke), ("pe", marks["v"])]
                if state.get("m_sc_prev"):
                    kv_w.append(("v", state["m_sc_prev"]))
                m_kv = op("vector", lambda e, bi=bi, b=banks[2]: e.tensor_tensor(
                    kvf[:, :], ktmp[:, bi * ST:(bi + 1) * ST], pb[b][:, :],
                    ALU.mult), waits=kv_w, inc="v")
                kevac_prev[bi] = m_kv
                init = 0.0 if t == 0 else carry[:, ec:ec + 1]
                sc_w = [("v", m_kv)]
                if state.get("m_vf_prev"):
                    sc_w.append(("v", state["m_vf_prev"]))
                if state.get("m_cy_prev"):
                    sc_w.append(("v", state["m_cy_prev"]))
                m_sc = op("vector", lambda e, init=init: e.tensor_tensor_scan(
                    kvt[:, :], zeros_t[:, :], kvf[:, :], init, ALU.add, ALU.add),
                    waits=sc_w, inc="v")
                state["m_sc_prev"] = m_sc
                wv_h = [("pe", marks["q"]), ("v", m_sc)]
                if vhf_prev[bi] is not None:
                    wv_h.append(("s", vhf_prev[bi]))
                m_vf = op("vector", lambda e, bi=bi, b=banks[0]:
                          e.scalar_tensor_tensor(
                              out=vhf[:, bi * ST:(bi + 1) * ST], in0=kvt[:, :],
                              scalar=DENOM, in1=pb[b][:, :], op0=ALU.mult,
                              op1=ALU.mult), waits=wv_h, inc="v")
                state["m_vf_prev"] = m_vf
                whi_w = [("v", m_vf)]
                if ec == 0 and state["last_wo_pe"]:
                    whi_w.append(("pe", state["last_wo_pe"]))
                state["m_cy_prev"] = op("vector", lambda e, ec=ec: e.tensor_copy(
                    carry[:, ec:ec + 1], kvt[:, ST - 1:ST]),
                    waits=[("v", m_sc)], inc="v")
                m_vhhi = op("scalar", lambda e, ec=ec, bi=bi: e.copy(
                    chs(vh_hi, ec), vhf[:, bi * ST:(bi + 1) * ST]),
                    waits=whi_w, inc="s")
                vhf_prev[bi] = m_vhhi
                m_vhlo = op("vector", lambda e, ec=ec, bi=bi: e.tensor_tensor(
                    chs(vh_lo, ec), vhf[:, bi * ST:(bi + 1) * ST],
                    chs(vh_hi, ec), ALU.subtract), waits=[("s", m_vhhi)],
                    inc="v")
                vh_dc.append([("s", m_vhhi), ("v", m_vhlo)])
                for b in banks:
                    bank_free[b] = ("v", m_vf)
            state["last_qkv_pe"] = pair_consumed[-1]
            state["vh_s"], state["vh_v"] = m_vhhi, m_vhlo
            state["vh_dc"] = vh_dc
            if t == 0:
                dump("d_vhhi", vh_hi[:, :], [("s", m_vhhi), ("v", m_vhlo)])
                dump("d_vhlo", vh_lo[:, :], [("v", m_vhlo)])
            m_x2_last = None
            m_x2_dc = []
            for fc in range(NCH):
                bidx = 6 + fc % 2
                sh, slo, mld = load_pair("o", fc, "scalar" if t == 0 else "sync")
                mwo = mm_group(pb[bidx][:, :], sh, slo, mld, vh_hi, vh_lo,
                               bidx, kc_waits=state["vh_dc"])
                m_x2 = op("vector", lambda e, fc=fc, bidx=bidx:
                          e.scalar_tensor_tensor(
                              out=chs(x_t, fc), in0=pb[bidx][:, :],
                              scalar=ga_t[:, fc:fc + 1], in1=chs(x_t, fc),
                              op0=ALU.mult, op1=ALU.add),
                          waits=[("pe", mwo), m_mod_fin[2]], inc="v")
                m_x2_last = m_x2
                m_x2_dc.append(("v", m_x2))
                bank_free[bidx] = ("v", m_x2)
            state["last_wo_pe"] = pair_consumed[-1]
            if t == 0:
                dump("d_x2", x_t[:, :], [("v", m_x2_last)])
            m_r2 = stats_phase(("v", m_x2_last), dep_dc=m_x2_dc)
            ln_apply(m_r2, c1b_t, sb_t, state["last_qkv_pe"])
            if t == 0:
                dump("d_r2", r_t[:, :], [("s", m_r2)])
                dump("d_rthi", rt_hi[:, 0:NCH * 8], [("s", m_ab_s)])
                dump("d_rtlo", rt_lo[:, 0:NCH * 8], [("v", m_rtlo)])
                dump("d_h2hi", h_hi[:, :], [("s", state["h_s"]), ("v", state["h_v"])])
                dump("d_h2lo", h_lo[:, :], [("v", state["h_v"])])
            rmark = None
            for dc in range(NCH):
                for pi, (ws, mvv) in enumerate(((rt_hi, h_hi), (rt_hi, h_lo),
                                                (rt_lo, h_hi))):
                    first = dc == 0 and pi == 0
                    last = dc == NCH - 1 and pi == 2
                    wts = list(state["h_dc"][dc]) if pi == 0 else []
                    if first:
                        wts += [bank_free[2], ("v", m_rtlo)]
                    rmark = op("tensor", lambda e, dc=dc, ws=ws, mvv=mvv,
                               first=first, last=last: e.matmul(
                                   pb[2][:, :], ws[:, dc * 128:(dc + 1) * 128],
                                   chs(mvv, dc), start=first, stop=last),
                               waits=wts, inc=("pe" if last else None))
            m_lg = op("scalar", lambda e: e.copy(lg_t[:, :], pb[2][0:8, :]),
                      waits=[("pe", rmark)], inc="s")
            bank_free[2] = ("s", m_lg)
            if t == 0:
                dump("d_lg", lg_t[:, :], [("s", m_lg)])
            m_m8_last = None
            m_m8_prev = None
            for sl in range(4):
                bidx = 3 + sl % 2
                mtr = op("tensor", lambda e, sl=sl, bidx=bidx: e.transpose(
                    pb[bidx][:, 0:8], lg_t[:, sl * 128:(sl + 1) * 128],
                    ident[0:8, 0:8]),
                    waits=[("s", m_lg), bank_free[bidx]], inc="pe")
                m_lt = op("scalar", lambda e, sl=sl, bidx=bidx: e.copy(
                    lt_t[:, sl * 8:(sl + 1) * 8], pb[bidx][:, 0:8]),
                    waits=[("pe", mtr)], inc="s")
                bank_free[bidx] = ("s", m_lt)
                m_mx = op("vector", lambda e, sl=sl: e.reduce_max(
                    lm_t[:, sl:sl + 1], lt_t[:, sl * 8:(sl + 1) * 8],
                    mybir.AxisListType.X), waits=[("s", m_lt)], inc="v")
                m_df = op("scalar", lambda e, sl=sl: e.activation(
                    out=dt_t[:, sl * 8:(sl + 1) * 8],
                    in_=lt_t[:, sl * 8:(sl + 1) * 8], func=AF.Identity,
                    scale=sneg_t[:, 0:1], bias=lm_t[:, sl:sl + 1]),
                    waits=[("v", m_mx), ("v", m_sneg), ("s", m_lt)], inc="s")
                m_mt = op("vector", lambda e, sl=sl: e.tensor_scalar(
                    mt_t[:, sl * 8:(sl + 1) * 8], dt_t[:, sl * 8:(sl + 1) * 8],
                    0.0, None, ALU.is_equal), waits=[("s", m_df)], inc="v")
                tb_w = [("v", m_mt)]
                if sl == 0:
                    tb_w.append(bank_free[5])
                else:
                    tb_w.append(("s", m_m8_prev))
                mtb = op("tensor", lambda e, sl=sl: e.transpose(
                    pb[5][0:8, 0:128], mt_t[:, sl * 8:(sl + 1) * 8],
                    ident[:, :]), waits=tb_w, inc="pe")
                m_m8 = op("scalar", lambda e, sl=sl: e.copy(
                    m8_t[:, sl * 128:(sl + 1) * 128], pb[5][0:8, 0:128]),
                    waits=[("pe", mtb)], inc="s")
                m_m8_prev = m_m8
                m_m8_last = m_m8
            bank_free[5] = ("s", m_m8_last)
            m_mex = op("tensor", lambda e: e.matmul(
                pb[6][:, :], emat[:, :], m8_t[:, :], start=True, stop=True),
                waits=[("s", m_m8_last), bank_free[6], ("sm3", m_e7)], inc="pe")
            m_mf = op("scalar", lambda e: e.copy(mf_t[:, :], pb[6][:, :]),
                      waits=[("pe", m_mex)], inc="s")
            bank_free[6] = ("s", m_mf)
            if t == 0:
                dump("d_mf", mf_t[:, :], [("s", m_mf)])
            amark = None
            for dc in range(NCH):
                for pi, (ws, mvv) in enumerate(((a_hi, h_hi), (a_hi, h_lo),
                                                (a_lo, h_hi))):
                    first = dc == 0 and pi == 0
                    last = dc == NCH - 1 and pi == 2
                    wts = list(state["h_dc"][dc]) if pi == 0 else []
                    if first:
                        wts += [bank_free[7], ("v", m_ab_v)]
                    amark = op("tensor", lambda e, dc=dc, ws=ws, mvv=mvv,
                               first=first, last=last: e.matmul(
                                   pb[7][:, :], ws[:, dc * 128:(dc + 1) * 128],
                                   chs(mvv, dc), start=first, stop=last),
                               waits=wts, inc=("pe" if last else None))
            m_am = op("vector", lambda e: e.tensor_tensor(
                amf[:, :], pb[7][:, :], mf_t[:, :], ALU.mult),
                waits=[("pe", amark), ("s", m_mf)], inc="v")
            bank_free[7] = ("v", m_am)
            if t == 0:
                dump("d_amf", amf[:, :], [("v", m_am)])
            m_amh = op("scalar", lambda e: e.copy(am_hi[:, :], amf[:, :]),
                       waits=[("v", m_am)], inc="s")
            m_aml = op("vector", lambda e: e.tensor_tensor(
                am_lo[:, :], amf[:, :], am_hi[:, :], ALU.subtract),
                waits=[("s", m_amh), ("v", m_am)], inc="v")
            tb_prev = state.setdefault("tb_prev", [None, None])
            ot_prev = state.setdefault("ot_prev", [None, None])
            for oc in range(NCH):
                bidx = oc % 3
                sh, slo, mld = load_pair("b", oc, "scalar" if t == 0 else "sync")

                def mk_extra(i, oc=oc):
                    def f(e, pbank_ap, last):
                        ws, mvv = ((bw_hi, am_hi), (bw_hi, am_lo),
                                   (bw_lo, am_hi))[i]
                        return e.matmul(pbank_ap,
                                        ws[:, oc * 128:(oc + 1) * 128],
                                        mvv[:, :], start=False, stop=last)
                    return f
                mb_ = mm_group(pb[bidx][:, :], sh, slo, mld, h_hi, h_lo, bidx,
                               extra_waits=[("s", m_amh), ("v", m_aml)],
                               extra_mms=[mk_extra(0), mk_extra(1),
                                          mk_extra(2)],
                               kc_waits=state["h_dc"])
                bi = oc % 2
                wtb = [("pe", mb_)]
                if tb_prev[bi] is not None:
                    wtb.append(("v", tb_prev[bi]))
                m_tb = op("scalar", lambda e, bi=bi, oc=oc, bidx=bidx:
                          e.activation(out=tb_t[:, bi * ST:(bi + 1) * ST],
                                       in_=pb[bidx][:, :], func=AF.Identity,
                                       bias=bb_t[:, oc:oc + 1], scale=1.0),
                          waits=wtb, inc="s")
                wot = [("s", m_tb)]
                if ot_prev[bi] is not None:
                    wot.append((f"os{bi}", ot_prev[bi]))
                m_ot = op("vector", lambda e, bi=bi, oc=oc:
                          e.scalar_tensor_tensor(
                              out=ot_t[:, bi * ST:(bi + 1) * ST],
                              in0=tb_t[:, bi * ST:(bi + 1) * ST],
                              scalar=gb_t[:, oc:oc + 1], in1=chs(x_t, oc),
                              op0=ALU.mult, op1=ALU.add),
                          waits=wot + [m_mod_fin[5]], inc="v")
                tb_prev[bi] = m_ot
                bank_free[bidx] = ("s", m_tb)
                m_og = op("scalar", lambda e, t=t, oc=oc, bi=bi: e.dma_start(
                    out=outTr[:, oc, t * ST:(t + 1) * ST],
                    in_=ot_t[:, bi * ST:(bi + 1) * ST]),
                    waits=[("v", m_ot)], inc=f"os{bi}")
                ot_prev[bi] = m_og
                state["last_out_v"] = m_ot
                if t + 1 < nt:
                    mx_ = op("scalar", lambda e, t=t, oc=oc: e.dma_start(
                        out=chs(x_t, oc), in_=xTr[:, oc, (t + 1) * ST:(t + 2) * ST]),
                        waits=[("v", m_ot)], inc=f"xc{oc}")
                    x_dc[t + 1][oc] = (f"xc{oc}", mx_)
            state["last_h2use_pe"] = pair_consumed[-1]

        final_gd = CNT["gd"]
        final_sd = CNT["sd"]
        final_os = (CNT["os0"], CNT["os1"])

        def emit(engname, engobj):
            memo = {k: 0 for k in SEM_ENG}
            for (eng, fn, waits, inc, amt) in OPS:
                if eng != engname:
                    continue
                for (sname, val) in waits:
                    if val > memo[sname]:
                        engobj.wait_ge(sems[sname], val)
                        memo[sname] = val
                ins = fn(engobj)
                if inc is not None:
                    ins.then_inc(sems[inc], amt)

        @block.sync
        def _(sync):
            emit("sync", sync)
            sync.wait_ge(sems["sd"], final_sd)
            sync.wait_ge(sems["os0"], final_os[0])
            sync.wait_ge(sems["os1"], final_os[1])

        @block.gpsimd
        def _(gpsimd):
            emit("gpsimd", gpsimd)
            gpsimd.wait_ge(sems["gd"], final_gd)

        @block.vector
        def _(vector):
            emit("vector", vector)

        @block.scalar
        def _(scalar):
            emit("scalar", scalar)

        @block.tensor
        def _(tensor):
            emit("tensor", tensor)

    return nc


_NC_CACHE = {}


def _get_nc():
    if "nc" not in _NC_CACHE:
        _NC_CACHE["nc"] = build_nc()
    return _NC_CACHE["nc"]


def make_in_maps(L0_attn_x, L0_attn_Wq, L0_attn_Wk, L0_attn_Wv, L0_attn_Wo,
                 L0_ffn_W_base, L0_ffn_b_base, L0_ffn_router_W, L0_ffn_A,
                 L0_ffn_Bw, L0_c, L0_wmod):
    B = L0_attn_x.shape[0]
    asc = np.ascontiguousarray
    wqT = asc(np.asarray(L0_attn_Wq).T)
    wkT = asc(np.asarray(L0_attn_Wk).T)
    wvT = asc(np.asarray(L0_attn_Wv).T)
    woT = asc(np.asarray(L0_attn_Wo).T)
    wbT = asc(np.asarray(L0_ffn_W_base).T)
    wmod_a = asc(np.asarray(L0_wmod))
    rts_np = np.zeros((128, NCH, 128), dtype=np.float32)
    rts_np[:, :, 0:8] = (np.asarray(L0_ffn_router_W).T.reshape(NCH, 128, 8)
                         .transpose(1, 0, 2))
    rts = asc(rts_np.reshape(128, NCH * 128))
    ats = asc(np.asarray(L0_ffn_A).reshape(128, D).T.reshape(NCH, 128, 128)
              .transpose(1, 0, 2).reshape(128, NCH * 128))
    bwts = asc(np.asarray(L0_ffn_Bw).transpose(0, 2, 1).reshape(128, D))
    bb = asc(np.asarray(L0_ffn_b_base).reshape(NCH, 128).T)
    cc = np.asarray(L0_c)
    id_np = np.eye(128, dtype=np.float32)
    em_np = (np.repeat(np.eye(8, dtype=np.float32), 16, axis=1) * 2.0
             ).astype(ml_dtypes.bfloat16)
    in_maps = []
    for b in range(B):
        in_maps.append({
            "xT": asc(np.asarray(L0_attn_x[b]).T),
            "wq": wqT, "wk": wkT, "wv": wvT, "wo": woT, "wb": wbT,
            "wmod": wmod_a, "rts": rts, "ats": ats, "bwts": bwts,
            "c_in": asc(cc[b].reshape(NCH, 128).T),
            "bb_in": bb, "id_in": id_np, "em_in": em_np,
        })
    return in_maps


def kernel(L0_attn_x, L0_attn_Wq, L0_attn_Wk, L0_attn_Wv, L0_attn_Wo,
           L0_ffn_W_base, L0_ffn_b_base, L0_ffn_router_W, L0_ffn_A, L0_ffn_Bw,
           L0_c, L0_wmod):
    in_maps = make_in_maps(L0_attn_x, L0_attn_Wq, L0_attn_Wk, L0_attn_Wv,
                           L0_attn_Wo, L0_ffn_W_base, L0_ffn_b_base,
                           L0_ffn_router_W, L0_ffn_A, L0_ffn_Bw, L0_c, L0_wmod)
    nc = _get_nc()
    res = run_bass_kernel_spmd(nc, in_maps, core_ids=list(range(len(in_maps))))
    out = np.stack([np.asarray(r["outT"]).T for r in res.results], axis=0)
    return out
